# version 12 (final)
# baseline (speedup 1.0000x reference)
import os
import numpy as np
import ml_dtypes
import concourse.bass as bass
import concourse.mybir as mybir
from concourse.bass_utils import run_bass_kernel_spmd
from concourse.alu_op_type import AluOpType as OP
from contextlib import ExitStack

F32 = mybir.dt.float32
BF16 = mybir.dt.bfloat16
F16 = mybir.dt.float16
I32 = mybir.dt.int32
AF = mybir.ActivationFunctionType
AX = mybir.AxisListType

NCORES = 8
B, L, D, N, RK, II = 4, 2048, 2048, 16, 128, 8192
T = L // NCORES
TOK = B * T
NS = D // 128
NQ = TOK // 128
NPIECE = B * NS
CLIPW = float(np.float32(1.4999999))
NW = float(D * II)

DEBUG = bool(int(os.environ.get("KDEBUG", "0")))

V, S, TE, G, Y = "v", "s", "t", "g", "y"
ENGSEM = {V: "v", S: "s", TE: "t", G: "g", Y: "ds"}


class Bld:
    def __init__(self):
        self.streams = {e: [] for e in (V, S, TE, G, Y)}
        self.cnt = {}
        self.pending = {e: () for e in (V, S, TE, G, Y)}

    COMPUTE_SEMS = ("v", "s", "t", "g")

    def op(self, eng, fn, waits=(), sig=None, inc=1, fence=False):
        if fence and sig is None:
            sig = ENGSEM[eng]
        if sig in ("v", "s", "t", "g"):
            fence = True
        waits = tuple(w for w in waits if w is not None)
        if self.pending[eng]:
            waits = waits + self.pending[eng]
            self.pending[eng] = ()
        tok = None
        if sig is not None:
            self.cnt[sig] = self.cnt.get(sig, 0) + inc
            tok = (sig, self.cnt[sig])
        self.streams[eng].append([fn, waits, sig, inc, fence, tok])
        return tok

    def dma(self, eng, out, in_, waits=(), klass=None, **kw):
        sig = klass or ("dg" if eng == G else "ds")
        fn = (lambda o, i, k: (lambda e: e.dma_start(out=o, in_=i, **k)))(out, in_, kw)
        return self.op(eng, fn, waits, sig, 16)

    def red(self, eng, fn, waits=(), sig=None):
        return self.op(eng, fn, waits, sig if sig else ENGSEM[eng], 1, fence=True)

    def cc(self, kind, opk, ins, outs, waits=(), sem="cc"):
        fn = (lambda k2, o2, i2, u2: (lambda e: e.collective_compute(
            k2, o2, replica_groups=[list(range(NCORES))], ins=i2, outs=u2)))(
                kind, opk, ins, outs)
        return self.op(G, fn, waits, sem, 1)

    def barrier(self):
        toks = {}
        for e in (V, S, TE, G, Y):
            st = self.streams[e]
            if not st:
                continue
            fn, waits, sig, inc, fence, tok = st[-1]
            if sig is None:
                sig = ENGSEM[e]
                inc = 16 if sig in ("ds", "dg") else 1
                self.cnt[sig] = self.cnt.get(sig, 0) + inc
                tok = (sig, self.cnt[sig])
                st[-1] = [fn, waits, sig, inc, fence, tok]
            toks[e] = tok
        for e in (V, S, TE, G, Y):
            self.pending[e] = self.pending[e] + tuple(
                t for ee, t in toks.items()
                if ee != e or e in (G, Y))

    def play(self, blk, sems):
        Y_SEMS = ("ds", "sri", "bcd", "dsw", "dlw", "bcb", "lnw", "drw", "dg")
        G_SEMS = ("cc", "cc0", "cc1", "cc2", "cc3",
                  "ob", "xp", "dp", "wb")

        def body_final(name, eng):
            for sn in (Y_SEMS if name == Y else G_SEMS if name == G else ()):
                if self.cnt.get(sn, 0):
                    eng.wait_ge(sems[sn], self.cnt[sn])

        def mk2(name):
            def body(eng):
                for fn, waits, sig, inc, fence, tok in self.streams[name]:
                    for (sn, val) in waits:
                        eng.wait_ge(sems[sn], val)
                    inst = fn(eng)
                    if sig is not None:
                        inst.then_inc(sems[sig], inc)
                        if fence:
                            eng.wait_ge(sems[tok[0]], tok[1])
                body_final(name, eng)
            return body
        blk.sync(mk2(Y))
        blk.vector(mk2(V))
        blk.scalar(mk2(S))
        blk.tensor(mk2(TE))
        blk.gpsimd(mk2(G))


def mt(*toks):
    return tuple(t for t in toks if t is not None)


def build_program():
    nc = bass.Bass("TRN2", target_bir_lowering=False, debug=False)

    dI = lambda name, shape, dt=F32: nc.dram_tensor(name, shape, dt, kind="ExternalInput").ap()
    x_d = dI("x", [TOK, D])
    xpwr_d = dI("xpwr", [RK, D])
    wbc_d = dI("wbc", [D, 2 * N])
    dtpT_d = dI("dtpT", [RK, D])
    dtb_d = dI("dtb", [128, NS])
    alog_d = dI("alog", [128, NS * N])
    drow_d = dI("drow", [1, D])
    obrow_d = dI("obrow", [1, D])
    woT_d = dI("woT", [D, D])
    lnw_d = dI("lnw", [1, II])
    wg_d = dI("wg_sh", [D // NCORES, II])
    wu_d = dI("wu_sh", [D // NCORES, II])
    wd_d = dI("wd_sh", [II // NCORES, D])
    mask_d = dI("maskr", [1, NCORES])
    const_d = dI("constr", [1, 8])
    ident_d = dI("ident", [128, 128], BF16)

    out_d = nc.dram_tensor("out", [TOK, D], F32, kind="ExternalOutput").ap()

    dbg = {}
    if DEBUG:
        def dO(name, shape, dt=F32):
            dbg[name] = nc.dram_tensor("dbg_" + name, shape, dt, kind="ExternalOutput").ap()
        dO("xn", [TOK, D]); dO("delta", [128, NS * TOK], BF16); dO("bc", [2 * N, TOK], BF16)
        dO("y", [128, NS * TOK], BF16); dO("h0", [128, NPIECE * N]); dO("qx", [TOK, D], BF16)
        dO("ffn", [TOK, D]); dO("hend", [128, NPIECE * N]); dO("pend", [128, NPIECE * N])
        dO("sw", [128, 8]); dO("hsp", [TOK, II], F16)
        dO("xl", [TOK, D]); dO("stats", [128, 32]); dO("wp", [128, 24])
        dO("strow", [1, 48]); dO("swre", [128, 48]); dO("lnsnap", [128, 8])
        dO("qgo", [128, II], BF16); dO("qxt", [128, NS * TOK], BF16)
        dO("scales", [128, 48]); dO("qhd", [128, 1024], BF16)
        dO("gtd", [128, 2048], BF16); dO("htd", [128, 2048], F16)
        dO("ssqp", [128, 128]); dO("hlmp", [128, 128])
        dO("lnwb", [128, II]); dO("dbcd", [128, D])
        dO("ht2d", [128, 1024], F16); dO("z1kd", [128, 1024])

    dS = lambda name, shape, dt: nc.dram_tensor(name, shape, dt).ap()
    bcd = dS("bcd", [2 * N, TOK], BF16)
    dxn_d = dS("dxn_d", [TOK, D], BF16)
    qxT_d = dS("qxT_d", [D, TOK], BF16)
    y_d = dS("y_d", [D, TOK], BF16)
    h_d = dS("h_d", [TOK, II], F16)
    sr_in = dS("sr_in", [1, 48], F32)
    sr_out = dS("sr_out", [1, 48], F32)
    qg_in = dS("qg_in", [D // NCORES, II], BF16)
    qg_out = dS("qg_out", [D, II], BF16)
    qu_in = dS("qu_in", [D // NCORES, II], BF16)
    qu_out = dS("qu_out", [D, II], BF16)
    qd_in = dS("qd_in", [II // NCORES, D], BF16)
    qd_out = dS("qd_out", [II, D], BF16)
    dA_d = dS("dA_d", [NPIECE * 128, N * T], BF16)
    ch_in = dS("ch_in", [128, 2 * NPIECE * N], F32)
    ch_out = dS("ch_out", [128 * NCORES, 2 * NPIECE * N], F32)

    b = Bld()

    cleanup = nc.cleanup_on_exit()
    cleanup.__enter__()
    _semnames = ("v", "s", "t", "g", "ds", "dg", "cc",
                 "ob", "lnw", "drw", "sri", "bcd", "xp", "dp", "wb",
                 "cc0", "cc1", "cc2", "cc3", "dsw", "dlw", "bcb")
    sems = {sn: nc.alloc_semaphore(sn, num=200 + i)
            for i, sn in enumerate(_semnames)}
    for sh in sems.values():
        nc.gpsimd.sem_clear(sh)
    nc.all_engine_barrier()
    with ExitStack() as es:
        ec = es.enter_context

        sb = lambda name, shape, dt: ec(nc.sbuf_tensor(name, shape, dt))
        apos = sb("apos", [128, NS * N], BF16)
        alog_s = sb("alog_s", [128, NS * N], F32)
        dtb_s = sb("dtb_s", [128, NS], F32)
        mask_s = sb("mask_s", [128, NCORES], F32)
        ident = sb("ident_s", [128, 128], BF16)
        ones1 = sb("ones1", [1, 128], BF16)
        onesf = sb("onesf", [128, 1], F32)
        obrow_s = sb("obrow_s", [1, D], BF16)
        bc_sb = sb("bc_sb", [2 * N, TOK], BF16)
        zero16 = sb("zero16", [128, N], BF16)
        xpwr_b = sb("xpwr_b", [128, D], BF16)
        dtpT_b = sb("dtpT_b", [128, D], BF16)
        wbcs = sb("wbcs", [128, NS * 2 * N], BF16)
        wparts2 = sb("wparts2", [128, 48], F32)
        swre = sb("swre", [128, 48], F32)
        swi = sb("swi", [128, 4], F32)
        swv = sb("swv", [128, 4], F32)
        lnm = sb("lnm", [128, NQ], F32)
        lnss = sb("lnss", [128, NQ], F32)
        lnvar = sb("lnvar", [128, NQ], F32)
        rstd = sb("rstd", [128, NQ], F32)
        nmr = sb("nmr", [128, NQ], F32)
        sxm = sb("sxm", [128, NQ], F32)
        sxs = sb("sxs", [128, NQ], F32)
        invg = sb("invg", [128, NQ], F32)
        invu = sb("invu", [128, NQ], F32)
        ssqp = sb("ssqp", [128, NQ * 16], F32)
        hlmp = sb("hlmp", [128, NQ * 16], F32)
        rmsr = sb("rmsr", [128, NQ], F32)
        qhsc = sb("qhsc", [128, NQ], F32)
        invd = sb("invd", [128, NQ], F32)
        t8a = sb("t8a", [128, NQ], F32)
        t8b = sb("t8b", [128, NQ], F32)
        t8c = sb("t8c", [128, NQ], F32)
        sumd = sb("sumd", [128, NPIECE], F32)
        csts = sb("csts", [128, 8], F32)
        lnsnap = sb("lnsnap_s", [128, NQ], F32)

        b.dma(Y, alog_s[:], alog_d)
        tk_cst = b.dma(Y, csts[:], const_d.broadcast_to((128, 8)), klass="lnw")
        b.dma(Y, dtb_s[:], dtb_d)
        tk_mask = b.dma(Y, mask_s[:], mask_d.broadcast_to((128, NCORES)), klass="drw")
        tk_id = b.dma(Y, ident[:], ident_d)
        tk_alog = tk_id
        tk_dtb = tk_id
        tk_ob = b.dma(G, obrow_s[:], obrow_d, klass="ob")
        tk_xp = b.dma(G, xpwr_b[:], xpwr_d, klass="xp")
        tk_dp = b.dma(G, dtpT_b[:], dtpT_d, klass="dp")
        tk_wb = b.dma(G, wbcs[:].rearrange("p (s w) -> p s w", s=NS),
                      wbc_d.rearrange("(s p) w -> p s w", p=128), klass="wb")

        tk_apos = b.op(S, lambda e: e.activation(out=apos[:], in_=alog_s[:], func=AF.Exp),
                       waits=[tk_alog], sig="s")
        b.op(V, lambda e: e.memset(ones1[:], 1.0))
        b.op(V, lambda e: e.memset(onesf[:], 1.0))
        tk_z16 = b.op(V, lambda e: e.memset(zero16[:], 0.0), sig="v")

        with ExitStack() as _ps2:
            xnT = _ps2.enter_context(nc.sbuf_tensor("xnT", [128, NS * TOK], BF16))
            delta = _ps2.enter_context(nc.sbuf_tensor("delta", [128, NS * TOK], BF16))
            with ExitStack() as _ps3:
                xfull = _ps3.enter_context(nc.sbuf_tensor("xfull", [128, NQ * D], F32))
                dbc = _ps3.enter_context(nc.sbuf_tensor("dbc", [128, D], F32))
                xnb = _ps3.enter_context(nc.sbuf_tensor("xnb", [128, D], BF16))
                dxnq = _ps3.enter_context(nc.sbuf_tensor("dxnq", [128, D], BF16))
                zx = _ps3.enter_context(nc.sbuf_tensor("zx", [128, D], F32))
                qxq = _ps3.enter_context(nc.sbuf_tensor("qxq", [128, D], BF16))
                sqb = _ps3.enter_context(nc.sbuf_tensor("sqb", [128, D], BF16))
                tsc = _ps3.enter_context(nc.sbuf_tensor("tsc", [128, 1024], BF16))
                ptA = _ps3.enter_context(nc.psum_tensor("ptA", [128, 1024], BF16))
                ptB = _ps3.enter_context(nc.psum_tensor("ptB", [128, 1024], BF16))
                tk_x = b.dma(Y, xfull[:].rearrange("p (q d) -> p q d", q=NQ),
                             x_d.rearrange("(q p) d -> p q d", p=128))
                tk_dbc = b.dma(Y, dbc[:], drow_d.broadcast_to((128, D)), klass="dg")
                if DEBUG:
                    b.dma(Y, dbg["dbcd"], dbc[:], waits=[tk_dbc])
                if DEBUG:
                    b.dma(Y, dbg["xl"].rearrange("(qq p) d -> p qq d", p=128),
                          xfull[:].rearrange("p (q d) -> p q d", q=NQ), waits=[tk_x])
                xq = lambda q: xfull[:, q * D:(q + 1) * D]
                tk_sq = None
                for q in range(NQ):
                    b.red(V, (lambda q2: lambda e: e.tensor_reduce(
                        out=lnm[:, q2:q2 + 1], in_=xq(q2), axis=AX.X, op=OP.add))(q),
                        waits=[tk_x] if q == 0 else ())
                    b.op(S, (lambda q2: lambda e: e.activation(
                        out=sqb[:], in_=xq(q2), func=AF.Square))(q),
                        waits=mt(tk_x if q == 0 else None, tk_sq), sig="s")
                    tk_sq = b.red(V, (lambda q2: lambda e: e.tensor_reduce(
                        out=lnss[:, q2:q2 + 1], in_=sqb[:], axis=AX.X, op=OP.add))(q),
                        waits=[("s", b.cnt["s"])], sig="v")
                b.op(V, lambda e: e.tensor_scalar(out=lnm[:], in0=lnm[:], scalar1=1.0 / D,
                                                  scalar2=None, op0=OP.mult),
                     sig="v", fence=True)
                tk_snap = b.op(V, lambda e: e.tensor_copy(out=lnsnap[:], in_=lnss[:]),
                               waits=[tk_sq], sig="v")
                if DEBUG:
                    b.dma(Y, dbg["lnsnap"], lnsnap[:], waits=[tk_snap])
                b.op(V, lambda e: e.tensor_scalar(out=lnvar[:], in0=lnss[:], scalar1=1.0 / D,
                                                  scalar2=None, op0=OP.mult),
                     sig="v", fence=True)
                b.op(V, lambda e: e.tensor_tensor(out=t8a[:], in0=lnm[:], in1=lnm[:],
                                                  op=OP.mult), sig="v", fence=True)
                tk_var = b.op(V, lambda e: e.tensor_tensor(out=lnvar[:], in0=lnvar[:],
                                                           in1=t8a[:], op=OP.subtract),
                              sig="v", fence=True)
                b.op(S, lambda e: e.activation(out=t8b[:], in_=lnvar[:], func=AF.Ln,
                                               bias=csts[:, 0:1]),
                     waits=[tk_var, tk_cst])
                tk_rstd = b.op(S, lambda e: e.activation(out=rstd[:], in_=t8b[:],
                                                         func=AF.Exp,
                                                         scale=csts[:, 2:3]), sig="s")
                tk_nmr = b.op(V, lambda e: e.scalar_tensor_tensor(
                    out=nmr[:], in0=lnm[:], scalar=-1.0, in1=rstd[:],
                    op0=OP.mult, op1=OP.mult), waits=[tk_rstd], sig="v", fence=True)
                if DEBUG:
                    b.dma(Y, dbg["stats"][:, 0:8], lnm[:], waits=[tk_nmr])
                    b.dma(Y, dbg["stats"][:, 8:16], lnss[:], waits=[tk_nmr])
                    b.dma(Y, dbg["stats"][:, 16:24], lnvar[:], waits=[tk_nmr])
                    b.dma(Y, dbg["stats"][:, 24:32], rstd[:], waits=[tk_nmr])
                tp_eva = {0: None, 1: None}
                tk_xnb_rd = None
                tk_qx_rd = None
                tk_dxn_dma = None
                for q in range(NQ):
                    tk_xn = b.op(S, (lambda q2: lambda e: e.activation(
                        out=xq(q2), in_=xq(q2), func=AF.Identity,
                        scale=rstd[:, q2:q2 + 1], bias=nmr[:, q2:q2 + 1]))(q),
                        waits=[tk_nmr], sig="s")
                    if DEBUG:
                        b.dma(Y, dbg["xn"].rearrange("(qq p) d -> p qq d", p=128)[:, q, :],
                              xq(q), waits=[tk_xn])
                    b.red(V, (lambda q2: lambda e: e.tensor_reduce(
                        out=sxm[:, q2:q2 + 1], in_=xq(q2), axis=AX.X, op=OP.max,
                        apply_absolute_value=True))(q), waits=[tk_xn])
                    b.op(V, (lambda q2: lambda e: e.tensor_scalar(
                        out=sxm[:, q2:q2 + 1], in0=sxm[:, q2:q2 + 1], scalar1=1e-5,
                        scalar2=None, op0=OP.max))(q), sig="v", fence=True)
                    b.red(V, (lambda q2: lambda e: e.reciprocal(
                        out=sxs[:, q2:q2 + 1], in_=sxm[:, q2:q2 + 1]))(q))
                    b.op(V, (lambda q2: lambda e: e.tensor_scalar(
                        out=sxs[:, q2:q2 + 1], in0=sxs[:, q2:q2 + 1], scalar1=127.0,
                        scalar2=None, op0=OP.mult))(q), sig="v", fence=True)
                    b.op(V, (lambda q2: lambda e: e.tensor_scalar(
                        out=zx[:], in0=xq(q2), scalar1=sxs[:, q2:q2 + 1], scalar2=None,
                        op0=OP.mult))(q), waits=[tk_sq])
                    b.op(V, lambda e: e.tensor_copy(
                        out=zx[:].bitcast(I32), in_=zx[:]))
                    tk_qx = b.op(V, lambda e: e.tensor_copy(
                        out=qxq[:], in_=zx[:].bitcast(I32)),
                        waits=[tk_qx_rd], sig="v")
                    if DEBUG:
                        b.dma(Y, dbg["qx"].rearrange("(qq p) d -> p qq d", p=128)[:, q, :],
                              qxq[:], waits=[tk_qx])
                    tk_xnb = b.op(S, (lambda q2: lambda e: e.activation(
                        out=xnb[:], in_=xq(q2), func=AF.Copy))(q),
                        waits=[tk_xnb_rd], sig="s")
                    tk_dxn = b.op(V, lambda e: e.tensor_tensor(
                        out=dxnq[:], in0=xnb[:], in1=dbc[:], op=OP.mult),
                        waits=[tk_xnb, tk_dbc, tk_dxn_dma], sig="v")
                    tk_dxn_dma = b.dma(
                        Y, dxn_d.rearrange("(qq p) d -> p qq d", p=128)[:, q, :],
                        dxnq[:], waits=[tk_dxn])
                    for half in range(2):
                        pt = [ptA, ptB][half]
                        tts = None
                        for si in range(8):
                            s = half * 8 + si
                            tts = b.op(TE, (lambda s2, si2, p2: lambda e: e.transpose(
                                p2[:, si2 * 128:(si2 + 1) * 128],
                                xnb[:, s2 * 128:(s2 + 1) * 128], ident[:]))(s, si, pt),
                                waits=mt(tk_xnb,
                                         tk_id if q == 0 and si == 0 else None,
                                         tp_eva[half] if si == 0 else None),
                                sig="t" if si == 7 else None)
                        tev = b.op(V, (lambda p2, q2, h2: lambda e: e.tensor_copy(
                            out=xnT[:].rearrange("p (s j) -> p s j", s=NS)
                                [:, h2 * 8:(h2 + 1) * 8, q2 * 128:(q2 + 1) * 128],
                            in_=p2[:].rearrange("p (s j) -> p s j", s=8)))(pt, q, half),
                            waits=[tts], sig="v")
                        tp_eva[half] = tev
                    tk_xnb_rd = tp_eva[1]
                    qx_ev = None
                    for half in range(2):
                        pt = [ptA, ptB][half]
                        tts = None
                        for si in range(8):
                            s = half * 8 + si
                            tts = b.op(TE, (lambda s2, si2, p2: lambda e: e.transpose(
                                p2[:, si2 * 128:(si2 + 1) * 128],
                                qxq[:, s2 * 128:(s2 + 1) * 128], ident[:]))(s, si, pt),
                                waits=mt(tk_qx if si == 0 else None,
                                         tp_eva[half] if si == 0 else None),
                                sig="t" if si == 7 else None)
                        tev = b.op(V, (lambda p2: lambda e: e.tensor_copy(
                            out=tsc[:], in_=p2[:]))(pt),
                            waits=mt(tts, qx_ev), sig="v")
                        qx_ev = b.dma(
                            Y, qxT_d.rearrange("(s dp) j -> dp s j", dp=128)
                                [:, half * 8:(half + 1) * 8, q * 128:(q + 1) * 128],
                            tsc[:].rearrange("p (s j) -> p s j", s=8), waits=[tev])
                        tp_eva[half] = tev
                    tk_qx_rd = qx_ev
                p1_v_last = tev
                b.barrier()

            _arena = ExitStack()
            wa0 = _arena.enter_context(nc.sbuf_tensor("wa0", [128, 1024], F32))
            wa1 = _arena.enter_context(nc.sbuf_tensor("wa1", [128, 1024], F32))
            zq = _arena.enter_context(nc.sbuf_tensor("zq", [128, 1024], F32))
            qq = _arena.enter_context(nc.sbuf_tensor("qq", [128, 1024], BF16))
            st_row = _arena.enter_context(nc.sbuf_tensor("st_row", [1, 48], F32))
            stat_specs = [(wg_d, 2, 8), (wu_d, 2, 8), (wd_d, 8, 2)]
            with ExitStack() as _ps1:
                ps_st = _ps1.enter_context(nc.psum_tensor("ps_st", [1, 48], F32))
                wtile = [wa0, wa1]
                freetok = [None, None]
                abstoks = []
                for mi, (wdram, npt, ncc) in enumerate(stat_specs):
                    src = wdram.rearrange("(a p) i -> p a i", p=128)
                    for pt in range(npt):
                        for ccx in range(ncc):
                            buf = len(abstoks) % 2
                            tl = b.dma(Y, wtile[buf][:],
                                       src[:, pt, ccx * 1024:(ccx + 1) * 1024],
                                       waits=[freetok[buf]])
                            tka = b.op(S, (lambda bu: lambda e: e.activation(
                                out=wtile[bu][:], in_=wtile[bu][:], func=AF.Abs))(buf),
                                waits=[tl], sig="s")
                            tok = b.red(V, (lambda bu, co: lambda e: e.tensor_reduce(
                                out=wparts2[:, co:co + 1], in_=wtile[bu][:],
                                axis=AX.X, op=OP.add))(
                                    buf, mi * 16 + pt * ncc + ccx),
                                waits=[tka], sig="v")
                            freetok[buf] = tok
                            abstoks.append(tok)
                tmm = b.op(TE, lambda e: e.matmul(ps_st[:], onesf[:], wparts2[:],
                                                  start=True, stop=True),
                           waits=[abstoks[-1], abstoks[-2]], sig="t")
                tcp = b.op(V, lambda e: e.tensor_copy(out=st_row[:], in_=ps_st[:]),
                           waits=[tmm], sig="v")
                tst = b.dma(Y, sr_in, st_row[:], waits=[tcp], klass="sri")
                if DEBUG:
                    b.dma(Y, dbg["wp"], wparts2[:, 0:24], waits=[tcp])
                    b.dma(Y, dbg["strow"], st_row[:], waits=[tcp])
                tar = b.cc("AllReduce", OP.add, [sr_in.opt()], [sr_out.opt()], waits=[tst],
                       sem="cc0")

            agtok = {}
            if True:
                wb0, wb1 = wa0, wa1
                tswl = b.dma(Y, swre[:], sr_out.broadcast_to((128, 48)), waits=[tar],
                             klass="dsw")
                if DEBUG:
                    b.dma(Y, dbg["swre"], swre[:], waits=[tswl])
                b.red(V, lambda e: e.tensor_reduce(
                    out=swi[:, 0:3],
                    in_=swre[:].rearrange("p (m k) -> p m k", m=3),
                    axis=AX.X, op=OP.add), waits=[tswl])
                b.op(V, lambda e: e.tensor_scalar(out=swi[:, 0:3], in0=swi[:, 0:3],
                                                  scalar1=1.0 / NW, scalar2=1e-5,
                                                  op0=OP.mult, op1=OP.max),
                     sig="v", fence=True)
                tk_swi = b.red(V, lambda e: e.reciprocal(out=swv[:, 0:3], in_=swi[:, 0:3]),
                              sig="v")
                if DEBUG:
                    b.dma(Y, dbg["sw"][:, 0:4], swi[:, 0:4], waits=[tk_swi])
                wtile = [wb0, wb1]
                freetok = [None, None]
                quant_specs = [(wg_d, qg_in, qg_out, 2, 8, 0),
                               (wu_d, qu_in, qu_out, 2, 8, 1),
                               (wd_d, qd_in, qd_out, 8, 2, 2)]
                laststore = None
                nq_ = 0
                for (wdram, qdram, qodram, npt, ncc, mi) in quant_specs:
                    src = wdram.rearrange("(a p) i -> p a i", p=128)
                    dst = qdram.rearrange("(a p) i -> p a i", p=128)
                    for pt in range(npt):
                        for ccx in range(ncc):
                            buf = nq_ % 2
                            tl = b.dma(Y, wtile[buf][:],
                                       src[:, pt, ccx * 1024:(ccx + 1) * 1024],
                                       waits=mt(freetok[buf],
                                                tk_swi if nq_ == 0 else None))
                            t1 = b.op(V, (lambda bu, m: lambda e: e.tensor_scalar(
                                out=zq[:], in0=wtile[bu][:], scalar1=swv[:, m:m + 1],
                                scalar2=CLIPW, op0=OP.mult, op1=OP.min))(buf, mi),
                                waits=[tl], sig="v")
                            freetok[buf] = t1
                            b.op(V, lambda e: e.tensor_scalar(
                                out=zq[:], in0=zq[:], scalar1=-CLIPW, scalar2=None,
                                op0=OP.max))
                            b.op(V, lambda e: e.tensor_copy(
                                out=zq[:].bitcast(I32), in_=zq[:]))
                            t4 = b.op(V, lambda e: e.tensor_copy(
                                out=qq[:], in_=zq[:].bitcast(I32)), sig="v")
                            laststore = b.dma(Y, dst[:, pt, ccx * 1024:(ccx + 1) * 1024],
                                              qq[:], waits=[t4, laststore])
                            nq_ += 1
                    agtok[mi] = b.cc("AllGather", OP.bypass, [qdram.opt()],
                                     [qodram.opt()], waits=[laststore],
                                     sem=f"cc{mi + 1}")

            xnT_v = xnT[:].rearrange("p (s j) -> p s j", s=NS)
            with ExitStack() as _ps5:
                dtwTs = _ps5.enter_context(nc.sbuf_tensor("dtwTs", [128, D], BF16))
                expscr = _ps5.enter_context(nc.sbuf_tensor("expscr", [128, TOK], F32))
                pdt0 = _ps5.enter_context(nc.psum_tensor("pdt0", [128, 512], F32))
                pdt1 = _ps5.enter_context(nc.psum_tensor("pdt1", [128, 512], F32))
                pdt2 = _ps5.enter_context(nc.psum_tensor("pdt2", [128, 512], F32))
                pdt3 = _ps5.enter_context(nc.psum_tensor("pdt3", [128, 512], F32))
                pdl0 = _ps5.enter_context(nc.psum_tensor("pdl0", [128, 512], F32))
                pdl1 = _ps5.enter_context(nc.psum_tensor("pdl1", [128, 512], F32))
                pbc = _ps5.enter_context(nc.psum_tensor("pbc", [32, TOK], F32))
                pdt = [pdt0, pdt1, pdt2, pdt3]
                prev_sp = None
                prev_ev = None
                prev_mm2 = None
                for s in range(NS):
                    mm1 = None
                    for dc in range(NS):
                        mm1 = b.op(TE, (lambda s2, dc2: lambda e: e.matmul(
                            pdt[dc2 // 4][:, (dc2 % 4) * 128:(dc2 % 4 + 1) * 128],
                            xpwr_b[:, dc2 * 128:(dc2 + 1) * 128],
                            dtpT_b[:, s2 * 128:(s2 + 1) * 128],
                            start=True, stop=True))(s, dc),
                            waits=mt(tk_xp if s == 0 and dc == 0 else None,
                                     tk_dp if s == 0 and dc == 0 else None,
                                     p1_v_last if s == 0 and dc == 0 else None,
                                     prev_ev if dc == 0 else None),
                            sig="t" if dc == NS - 1 else None)
                    tev = None
                    for g4 in range(4):
                        tev = b.op(S, (lambda g2: lambda e: e.activation(
                            out=dtwTs[:, g2 * 512:(g2 + 1) * 512], in_=pdt[g2][:],
                            func=AF.Copy))(g4),
                            waits=mt(mm1 if g4 == 0 else None,
                                     prev_mm2 if g4 == 0 else None), sig="s")
                    prev_ev = tev
                    mm2b = None
                    for dc in range(NS):
                        b.op(TE, (lambda s2, dc2: lambda e: e.matmul(
                            pdl0[:], dtwTs[:, dc2 * 128:(dc2 + 1) * 128],
                            xnT_v[:, dc2, 0:512],
                            start=(dc2 == 0), stop=(dc2 == NS - 1)))(s, dc),
                            waits=mt(tev if dc == 0 else None,
                                     prev_sp if dc == 0 else None))
                        mm2b = b.op(TE, (lambda s2, dc2: lambda e: e.matmul(
                            pdl1[:], dtwTs[:, dc2 * 128:(dc2 + 1) * 128],
                            xnT_v[:, dc2, 512:1024],
                            start=(dc2 == 0), stop=(dc2 == NS - 1)))(s, dc),
                            sig="t" if dc == NS - 1 else None)
                    prev_mm2 = mm2b
                    sp = None
                    for hh, pp in ((0, pdl0), (1, pdl1)):
                        b.op(S, (lambda s2, h2, p2: lambda e: e.activation(
                            out=expscr[:, h2 * 512:(h2 + 1) * 512], in_=p2[:],
                            func=AF.Exp, bias=dtb_s[:, s2:s2 + 1]))(s, hh, pp),
                            waits=mt(mm2b if hh == 0 else None,
                                     tk_dtb if s == 0 and hh == 0 else None))
                        sp = b.op(S, (lambda s2, h2: lambda e: e.activation(
                            out=delta[:, s2 * TOK + h2 * 512: s2 * TOK + (h2 + 1) * 512],
                            in_=expscr[:, h2 * 512:(h2 + 1) * 512], func=AF.Ln,
                            bias=1.0))(s, hh), sig="s")
                    prev_sp = sp
                    if DEBUG:
                        b.dma(Y, dbg["delta"][:, s * TOK:(s + 1) * TOK],
                              delta[:, s * TOK:(s + 1) * TOK], waits=[sp])
                mmbc = None
                for s in range(NS):
                    for hh in range(2):
                        mmbc = b.op(TE, (lambda s2, h2: lambda e: e.matmul(
                            pbc[:, h2 * 512:(h2 + 1) * 512],
                            wbcs[:, s2 * 32:(s2 + 1) * 32],
                            xnT_v[:, s2, h2 * 512:(h2 + 1) * 512],
                            start=(s2 == 0), stop=(s2 == NS - 1)))(s, hh),
                            waits=[tk_wb] if s == 0 and hh == 0 else (), sig="t")
                tk_bcev = b.op(S, lambda e: e.activation(out=bc_sb[:], in_=pbc[:],
                                                         func=AF.Copy),
                               waits=[mmbc], sig="s")
                tk_bcd = b.dma(Y, bcd, bc_sb[:], waits=[tk_bcev], klass="bcd")
                if DEBUG:
                    b.dma(Y, dbg["bc"], bc_sb[:], waits=[tk_bcev])
                b.barrier()

            _arena.close()
            apv = apos[:].rearrange("p (s n) -> p s n", s=NS)
            dv = delta[:].rearrange("p (s j) -> p s j", s=NS)
            xv2 = xnT[:].rearrange("p (s j) -> p s j", s=NS)
            with ExitStack() as _ps6:
                y_sb = _ps6.enter_context(nc.sbuf_tensor("y_sb", [128, NS * TOK], BF16))
                zsc0 = _ps6.enter_context(nc.sbuf_tensor("zsc0", [128, N * T], BF16))
                zsc1 = _ps6.enter_context(nc.sbuf_tensor("zsc1", [128, N * T], BF16))
                dA0_ = _ps6.enter_context(nc.sbuf_tensor("dA0_", [128, N * T], BF16))
                dA1_ = _ps6.enter_context(nc.sbuf_tensor("dA1_", [128, N * T], BF16))
                dxb = _ps6.enter_context(nc.sbuf_tensor("dxb", [128, T], BF16))
                ub = _ps6.enter_context(nc.sbuf_tensor("ub", [128, N * T], BF16))
                hb = _ps6.enter_context(nc.sbuf_tensor("hb", [128, N * T], BF16))
                bbt = _ps6.enter_context(nc.sbuf_tensor("bbt", [128, N * T], BF16))
                cbt = _ps6.enter_context(nc.sbuf_tensor("cbt", [128, N * T], BF16))
                dA0b = _ps6.enter_context(nc.sbuf_tensor("dA0b", [128, N], BF16))
                t16b = _ps6.enter_context(nc.sbuf_tensor("t16b", [128, N], BF16))
                hend = _ps6.enter_context(nc.sbuf_tensor("hend", [128, NPIECE * N], F32))
                pend = _ps6.enter_context(nc.sbuf_tensor("pend", [128, NPIECE * N], F32))
                zp_gpe = _ps6.enter_context(nc.sbuf_tensor("zp_gpe", [128, NPIECE * N], F32))
                h0a = _ps6.enter_context(nc.sbuf_tensor("h0a", [128, NPIECE * N], F32))

                zscl = [zsc0, zsc1]
                dAl = [dA0_, dA1_]
                uv = ub[:].rearrange("p (n t) -> p n t", n=N)
                hv = hb[:].rearrange("p (n t) -> p n t", n=N)
                bbv = bbt[:].rearrange("p (n t) -> p n t", n=N)
                cbv = cbt[:].rearrange("p (n t) -> p n t", n=N)
                dAd_v = dA_d.rearrange("(k p) f -> p k f", p=128)

                piece_list = [(bi, s) for bi in range(B) for s in range(NS)]
                st = {"zfree": [None, None], "dAfree": [None, None],
                      "texp": [None] * NPIECE, "tz": [None] * NPIECE,
                      "tu": None, "ty": None, "tscan": None,
                      "tbb": {}, "tcb": {}}

                def emit_z(k):
                    bi, s = piece_list[k]
                    par = k % 2
                    dsl = dv[:, s, bi * T:(bi + 1) * T]
                    zvw = zscl[par][:].rearrange("p (n t) -> p n t", n=N)
                    st["tz"][k] = b.op(V, (lambda d2, s2, zz: lambda e: e.tensor_tensor(
                        out=zz, in0=d2.unsqueeze(1).broadcast_to((128, N, T)),
                        in1=apv[:, s2, :].unsqueeze(2).broadcast_to((128, N, T)),
                        op=OP.mult))(dsl, s, zvw),
                        waits=mt(st["zfree"][par]), sig="v")

                def emit_exp(k):
                    par = k % 2
                    st["texp"][k] = b.op(S, (lambda zz, dd: lambda e: e.activation(
                        out=dd[:], in_=zz[:], func=AF.Exp, scale=csts[:, 3:4]))(
                            zscl[par], dAl[par]),
                        waits=mt(st["tz"][k], st["dAfree"][par],
                                 tk_cst if k == 0 else None,
                                 tk_apos if k == 0 else None),
                        sig="s")
                    st["zfree"][par] = st["texp"][k]

                def consume(k, pass2):
                    bi, s = piece_list[k]
                    p = k
                    par = k % 2
                    dAb = dAl[par]
                    dsl = dv[:, s, bi * T:(bi + 1) * T]
                    xsl = xv2[:, s, bi * T:(bi + 1) * T]
                    if not pass2:
                        tst_ = b.dma(Y, dAd_v[:, p, :], dAb[:], waits=[st["texp"][k]])
                        tzero = b.op(V, (lambda dd: lambda e: e.tensor_copy(
                            out=dd[:].rearrange("p (n t) -> p n t", n=N)[:, :, 0:1],
                            in_=zero16[:].unsqueeze(2)))(dAb),
                            waits=mt(st["texp"][k], tst_,
                                     tk_z16 if k == 0 else None),
                            sig="v", fence=True)
                    else:
                        tld = st["texp"][k]
                        b.op(V, (lambda dd: lambda e: e.tensor_copy(
                            out=dA0b[:].unsqueeze(2),
                            in_=dd[:].rearrange("p (n t) -> p n t", n=N)[:, :, 0:1]))(dAb),
                            waits=mt(tld), sig="v", fence=True)
                        tzero = b.op(V, (lambda dd: lambda e: e.tensor_copy(
                            out=dd[:].rearrange("p (n t) -> p n t", n=N)[:, :, 0:1],
                            in_=zero16[:].unsqueeze(2)))(dAb), sig="v", fence=True)
                    b.op(V, (lambda d2, x2: lambda e: e.tensor_tensor(
                        out=dxb[:], in0=d2, in1=x2, op=OP.mult))(dsl, xsl))
                    tu = b.op(V, lambda e: e.tensor_tensor(
                        out=uv, in0=dxb[:].unsqueeze(1).broadcast_to((128, N, T)),
                        in1=bbv, op=OP.mult), waits=mt(st["tbb"].get(bi)), sig="v")
                    if pass2:
                        b.op(V, (lambda p2: lambda e: e.tensor_tensor(
                            out=t16b[:], in0=dA0b[:], in1=h0a[:, p2 * N:(p2 + 1) * N],
                            op=OP.mult))(p), sig="v", fence=True)
                        tu = b.op(V, lambda e: e.tensor_tensor(
                            out=uv[:, :, 0:1], in0=uv[:, :, 0:1],
                            in1=t16b[:].unsqueeze(2), op=OP.add), sig="v", fence=True)
                    st["tu"] = tu
                    tscan = b.red(V, (lambda dd: lambda e: e.tensor_tensor_scan(
                        out=hb[:], data0=dd[:], data1=ub[:], initial=0.0,
                        op0=OP.mult, op1=OP.add))(dAb), waits=mt(tzero))
                    st["tscan"] = tscan
                    st["dAfree"][par] = tscan
                    if not pass2:
                        tl2 = b.op(V, (lambda p2: lambda e: e.tensor_copy(
                            out=hend[:, p2 * N:(p2 + 1) * N].unsqueeze(2),
                            in_=hv[:, :, T - 1:T]))(p), sig="v", fence=True)
                        b.red(V, (lambda p2, d2: lambda e: e.tensor_reduce(
                            out=sumd[:, p2:p2 + 1], in_=d2, axis=AX.X,
                            op=OP.add))(p, dsl))
                        return tl2
                    else:
                        b.op(V, lambda e: e.tensor_tensor(out=hb[:], in0=hb[:],
                                                          in1=cbt[:], op=OP.mult),
                             waits=mt(st["tcb"].get(bi)))
                        b.op(V, lambda e: e.tensor_tensor(
                            out=hv[:, 0:8, :], in0=hv[:, 0:8, :], in1=hv[:, 8:16, :],
                            op=OP.add))
                        b.op(V, lambda e: e.tensor_tensor(
                            out=hv[:, 0:4, :], in0=hv[:, 0:4, :], in1=hv[:, 4:8, :],
                            op=OP.add))
                        b.op(V, lambda e: e.tensor_tensor(
                            out=hv[:, 0:2, :], in0=hv[:, 0:2, :], in1=hv[:, 2:4, :],
                            op=OP.add))
                        ty = b.op(V, (lambda s2, bi2: lambda e: e.tensor_tensor(
                            out=y_sb[:, s2 * TOK + bi2 * T: s2 * TOK + (bi2 + 1) * T],
                            in0=hv[:, 0, :], in1=hv[:, 1, :], op=OP.add))(s, bi),
                            sig="v")
                        st["ty"] = ty
                        return ty

                tlast = None
                for bi in range(B):
                    st["tbb"][bi] = b.dma(Y, bbv, bcd[0:N, bi * T:(bi + 1) * T]
                                          .unsqueeze(0).broadcast_to((128, N, T)),
                                          waits=mt(tk_bcd if bi == 0 else None,
                                                   st["tu"]), klass="bcb")
                    for s in range(NS):
                        k = bi * NS + s
                        if k == 0:
                            emit_z(0)
                            emit_exp(0)
                        if k + 1 < NPIECE:
                            emit_z(k + 1)
                            emit_exp(k + 1)
                        tlast = consume(k, False)
                tzp = None
                for p in range(NPIECE):
                    s = p % NS
                    tzp = b.op(V, (lambda p2, s2: lambda e: e.tensor_tensor(
                        out=zp_gpe[:, p2 * N:(p2 + 1) * N],
                        in0=sumd[:, p2:p2 + 1].broadcast_to((128, N)),
                        in1=apv[:, s2, :], op=OP.mult))(p, s),
                        sig="v" if p == NPIECE - 1 else None)
                tpe = b.op(S, lambda e: e.activation(out=pend[:], in_=zp_gpe[:],
                                                     func=AF.Exp,
                                                     scale=csts[:, 3:4]),
                           waits=mt(tzp, st["tscan"]), sig="s")
                the = b.dma(Y, ch_in[:, 0:NPIECE * N], hend[:], waits=[tlast])
                tpee = b.dma(Y, ch_in[:, NPIECE * N:], pend[:], waits=[tpe])
                if DEBUG:
                    b.dma(Y, dbg["hend"], hend[:], waits=[tlast])
                    b.dma(Y, dbg["pend"], pend[:], waits=[tpe])
                tch = b.cc("AllGather", OP.bypass, [ch_in.opt()], [ch_out.opt()],
                           waits=[the, tpee], sem="cc")
                gcur, ghe, gpe = pend, hend, zp_gpe
                chv = ch_out.rearrange("(r p) k -> r p k", p=128)
                b.op(V, lambda e: e.memset(h0a[:], 0.0))
                tg0 = b.dma(Y, gcur[:], chv[0, :, 0:NPIECE * N], waits=[tch])
                tck = b.op(V, lambda e: e.scalar_tensor_tensor(
                    out=h0a[:], in0=gcur[:], scalar=mask_s[:, 0:1], in1=h0a[:],
                    op0=OP.mult, op1=OP.add), waits=[tg0, tk_mask], sig="v")
                for r in range(1, NCORES - 1):
                    tg1 = b.dma(Y, ghe[:], chv[r, :, 0:NPIECE * N],
                                waits=mt(tch if r == 1 else None, tck))
                    tg2 = b.dma(Y, gpe[:], chv[r, :, NPIECE * N:],
                                waits=mt(tch if r == 1 else None,
                                         tpe if r == 1 else None, tck))
                    b.op(V, lambda e: e.tensor_tensor(out=gcur[:], in0=gpe[:],
                                                      in1=gcur[:], op=OP.mult),
                         waits=[tg1, tg2])
                    b.op(V, lambda e: e.tensor_tensor(out=gcur[:], in0=gcur[:],
                                                      in1=ghe[:], op=OP.add))
                    tck = b.op(V, (lambda r2: lambda e: e.scalar_tensor_tensor(
                        out=h0a[:], in0=gcur[:], scalar=mask_s[:, r2:r2 + 1],
                        in1=h0a[:], op0=OP.mult, op1=OP.add))(r), sig="v")
                if DEBUG:
                    b.dma(Y, dbg["h0"], h0a[:], waits=[tck])
                st["tu"] = tck
                st["tbb"] = {}
                st["tcb"] = {}
                for bi in range(B):
                    st["tbb"][bi] = b.dma(Y, bbv, bcd[0:N, bi * T:(bi + 1) * T]
                                          .unsqueeze(0).broadcast_to((128, N, T)),
                                          waits=mt(st["tu"]), klass="bcb")
                    st["tcb"][bi] = b.dma(Y, cbv, bcd[N:2 * N, bi * T:(bi + 1) * T]
                                          .unsqueeze(0).broadcast_to((128, N, T)),
                                          waits=mt(st["ty"],
                                                   tck if bi == 0 else None),
                                          klass="bcb")
                    for s in range(NS):
                        k = bi * NS + s
                        if k == 0:
                            st["texp"][0] = b.dma(Y, dAl[0][:], dAd_v[:, 0, :],
                                                  waits=mt(st["dAfree"][0], tck))
                        if k + 1 < NPIECE:
                            st["texp"][k + 1] = b.dma(
                                Y, dAl[(k + 1) % 2][:], dAd_v[:, k + 1, :],
                                waits=mt(st["dAfree"][(k + 1) % 2]))
                        consume(k, True)
                tysp = b.dma(Y, y_d.rearrange("(s dp) j -> dp s j", dp=128),
                             y_sb[:].rearrange("p (s j) -> p s j", s=NS),
                             waits=[st["ty"]])
                if DEBUG:
                    b.dma(Y, dbg["y"], y_sb[:], waits=[st["ty"]])
                b.barrier()

        with ExitStack() as _ps7:
            lnwbc = _ps7.enter_context(nc.sbuf_tensor("lnwbc", [128, II], F32))
            tk_lbc = b.dma(Y, lnwbc[:], lnw_d.broadcast_to((128, II)), klass="dlw")
            if DEBUG:
                b.dma(Y, dbg["lnwb"], lnwbc[:], waits=[tk_lbc])
            b.op(V, lambda e: e.scalar_tensor_tensor(
                out=invg[:], in0=sxm[:], scalar=1.0 / 127.0,
                in1=swi[:, 0:1].broadcast_to((128, NQ)), op0=OP.mult, op1=OP.mult),
                waits=[tk_swi], sig="v", fence=True)
            tk_inv = b.op(V, lambda e: e.scalar_tensor_tensor(
                out=invu[:], in0=sxm[:], scalar=1.0 / 127.0,
                in1=swi[:, 1:2].broadcast_to((128, NQ)), op0=OP.mult, op1=OP.mult),
                sig="v")

            with ExitStack() as _ps8:
                qxTs = _ps8.enter_context(nc.sbuf_tensor("qxTs", [128, NS * TOK], BF16))
                gs0 = _ps8.enter_context(nc.sbuf_tensor("gs0", [128, NS * 512], BF16))
                gs1 = _ps8.enter_context(nc.sbuf_tensor("gs1", [128, NS * 512], BF16))
                us0 = _ps8.enter_context(nc.sbuf_tensor("us0", [128, NS * 512], BF16))
                us1 = _ps8.enter_context(nc.sbuf_tensor("us1", [128, NS * 512], BF16))
                gt0 = _ps8.enter_context(nc.sbuf_tensor("gt0", [128, 2048], BF16))
                gt1 = _ps8.enter_context(nc.sbuf_tensor("gt1", [128, 2048], BF16))
                ht0 = _ps8.enter_context(nc.sbuf_tensor("ht0", [128, 2048], F16))
                ht1 = _ps8.enter_context(nc.sbuf_tensor("ht1", [128, 2048], F16))
                hls = _ps8.enter_context(nc.sbuf_tensor("hls", [128, 2048], F16))
                sqs2 = _ps8.enter_context(nc.sbuf_tensor("sqs2", [128, 512], F32))
                pg0 = _ps8.enter_context(nc.psum_tensor("pg0", [128, 512], F32))
                pg1 = _ps8.enter_context(nc.psum_tensor("pg1", [128, 512], F32))
                pg2 = _ps8.enter_context(nc.psum_tensor("pg2", [128, 512], F32))
                pg3 = _ps8.enter_context(nc.psum_tensor("pg3", [128, 512], F32))
                pu0 = _ps8.enter_context(nc.psum_tensor("pu0", [128, 512], F32))
                pu1 = _ps8.enter_context(nc.psum_tensor("pu1", [128, 512], F32))
                pu2 = _ps8.enter_context(nc.psum_tensor("pu2", [128, 512], F32))
                pu3 = _ps8.enter_context(nc.psum_tensor("pu3", [128, 512], F32))
                pg = [pg0, pg1, pg2, pg3]
                pu = [pu0, pu1, pu2, pu3]
                gstr = [gs0, gs1]
                ustr = [us0, us1]
                gtb = [gt0, gt1]
                htb = [ht0, ht1]
                tk_qxl = b.dma(Y, qxTs[:].rearrange("p (s j) -> p s j", s=NS),
                               qxT_d.rearrange("(s dp) j -> dp s j", dp=128),
                               waits=[tk_qx_rd])
                qxv = qxTs[:].rearrange("p (s j) -> p s j", s=NS)
                if DEBUG:
                    b.dma(Y, dbg["qgo"], qg_out[0:128, :], waits=[agtok[0]])
                    b.dma(Y, dbg["qxt"], qxTs[:], waits=[tk_qxl])
                gfree = [None, None]
                ufree = [None, None]
                gev_prev = None
                hstt_prev = None
                hread_prev = {0: None, 1: None}
                sqS_prev = {0: None, 1: None}
                sq_rd = None
                for chunk in range(16):
                    bufp = chunk % 2
                    tgl = b.dma(Y, gstr[bufp][:].rearrange("p (k j) -> p k j", k=NS),
                                qg_out.rearrange("(k p) i -> p k i", p=128)
                                [:, :, chunk * 512:(chunk + 1) * 512],
                                waits=mt(agtok[0] if chunk == 0 else None, gfree[bufp]))
                    tul = b.dma(Y, ustr[bufp][:].rearrange("p (k j) -> p k j", k=NS),
                                qu_out.rearrange("(k p) i -> p k i", p=128)
                                [:, :, chunk * 512:(chunk + 1) * 512],
                                waits=mt(agtok[1] if chunk == 0 else None, ufree[bufp]))
                    gsv = gstr[bufp][:].rearrange("p (k j) -> p k j", k=NS)
                    usv = ustr[bufp][:].rearrange("p (k j) -> p k j", k=NS)
                    for qh in range(2):
                        gp = (chunk * 2 + qh) % 2
                        mmg = None
                        for kd in range(NS):
                            for qq2 in range(4):
                                q = qh * 4 + qq2
                                mmg = b.op(TE, (lambda kd2, q2, qq3, gv: lambda e: e.matmul(
                                    pg[qq3][:], qxv[:, kd2, q2 * 128:(q2 + 1) * 128],
                                    gv[:, kd2, :],
                                    start=(kd2 == 0), stop=(kd2 == NS - 1)))(kd, q, qq2, gsv),
                                    waits=mt(tgl if kd == 0 and qq2 == 0 else None,
                                             tk_qxl if chunk + qh + kd + qq2 == 0 else None,
                                             gev_prev if kd == 0 and qq2 == 0 else None),
                                    sig="t" if kd == NS - 1 and qq2 == 3 else None)
                        gev = None
                        for qq2 in range(4):
                            q = qh * 4 + qq2
                            gev = b.op(S, (lambda q2, qq3, gb: lambda e: e.activation(
                                out=gtb[gb][:, qq3 * 512:(qq3 + 1) * 512], in_=pg[qq3][:],
                                func=AF.Silu, scale=invg[:, q2:q2 + 1]))(q, qq2, gp),
                                waits=mt(mmg if qq2 == 0 else None,
                                         tk_inv if chunk + qh + qq2 == 0 else None,
                                         hstt_prev if qq2 == 0 else None), sig="s")
                        gev_prev = gev
                        mmu = None
                        for kd in range(NS):
                            for qq2 in range(4):
                                q = qh * 4 + qq2
                                mmu = b.op(TE, (lambda kd2, q2, qq3, uv2: lambda e: e.matmul(
                                    pu[qq3][:], qxv[:, kd2, q2 * 128:(q2 + 1) * 128],
                                    uv2[:, kd2, :],
                                    start=(kd2 == 0), stop=(kd2 == NS - 1)))(kd, q, qq2, usv),
                                    waits=mt(tul if kd == 0 and qq2 == 0 else None,
                                             hstt_prev if kd == 0 and qq2 == 0 else None),
                                    sig="t" if kd == NS - 1 and qq2 == 3 else None)
                        if qh == 1:
                            gfree[bufp] = mmg
                            ufree[bufp] = mmu
                        hstt = None
                        for qq2 in range(4):
                            q = qh * 4 + qq2
                            hstt = b.op(V, (lambda q2, qq3, gb: lambda e: e.scalar_tensor_tensor(
                                out=htb[gb][:, qq3 * 512:(qq3 + 1) * 512], in0=pu[qq3][:],
                                scalar=invu[:, q2:q2 + 1],
                                in1=gtb[gb][:, qq3 * 512:(qq3 + 1) * 512],
                                op0=OP.mult, op1=OP.mult))(q, qq2, gp),
                                waits=mt(mmu if qq2 == 0 else None,
                                         gev if qq2 == 0 else None,
                                         hread_prev[gp] if qq2 == 0 else None,
                                         sqS_prev[gp] if qq2 == 0 else None), sig="v")
                        hstt_prev = hstt
                        if DEBUG and chunk == 0 and qh == 0:
                            b.dma(Y, dbg["gtd"], gtb[gp][:], waits=[hstt])
                            b.dma(Y, dbg["htd"], htb[gp][:], waits=[hstt])
                        hrd = None
                        for qq2 in range(4):
                            q = qh * 4 + qq2
                            tsq = b.op(S, (lambda q2, qq3, c2, gb: lambda e: e.activation(
                                out=sqs2[:], in_=htb[gb][:, qq3 * 512:(qq3 + 1) * 512],
                                func=AF.Square,
                                accum_out=ssqp[:, q2 * 16 + c2:q2 * 16 + c2 + 1]))(
                                    q, qq2, chunk, gp),
                                waits=mt(hstt if qq2 == 0 else None), sig="s")
                            sq_rd = tsq
                            hrd = b.dma(Y, h_d.rearrange("(qq p) i -> p qq i", p=128)
                                        [:, q, chunk * 512:(chunk + 1) * 512],
                                        htb[gp][:, qq2 * 512:(qq2 + 1) * 512],
                                        waits=[hstt] if qq2 == 0 else ())
                        b.op(V, (lambda c2, gb: lambda e: e.tensor_tensor(
                            out=hls[:].rearrange("p (g j) -> p g j", g=4),
                            in0=htb[gb][:].rearrange("p (g j) -> p g j", g=4),
                            in1=lnwbc[:, c2 * 512:(c2 + 1) * 512].unsqueeze(1)
                            .broadcast_to((128, 4, 512)), op=OP.mult))(chunk, gp),
                            waits=mt(tk_lbc if chunk + qh == 0 else None))
                        b.red(V, (lambda c2, q0: lambda e: e.tensor_reduce(
                            out=hlmp[:].rearrange("p (q c) -> p q c", q=NQ)
                            [:, q0:q0 + 4, c2:c2 + 1],
                            in_=hls[:].rearrange("p (g j) -> p g j", g=4),
                            axis=AX.X, op=OP.max,
                            apply_absolute_value=True))(chunk, qh * 4))
                        hread_prev[gp] = hrd
                        sqS_prev[gp] = tsq
                b.barrier()

            b.red(V, lambda e: e.tensor_reduce(
                out=t8a[:], in_=ssqp[:].rearrange("p (q c) -> p q c", q=NQ),
                axis=AX.X, op=OP.add), waits=mt(sq_rd))
            b.op(V, lambda e: e.tensor_scalar(out=t8a[:], in0=t8a[:], scalar1=1.0 / II,
                                              scalar2=None, op0=OP.mult),
                 sig="v", fence=True)
            tk_v2 = b.red(V, lambda e: e.tensor_reduce(
                out=t8c[:], in_=hlmp[:].rearrange("p (q c) -> p q c", q=NQ),
                axis=AX.X, op=OP.max), sig="v")
            b.op(S, lambda e: e.activation(out=t8b[:], in_=t8a[:], func=AF.Ln,
                                           bias=csts[:, 1:2]), waits=[tk_v2])
            tk_rms = b.op(S, lambda e: e.activation(out=rmsr[:], in_=t8b[:], func=AF.Exp,
                                                    scale=csts[:, 2:3]), sig="s")
            b.op(V, lambda e: e.tensor_tensor(out=t8b[:], in0=rmsr[:], in1=t8c[:],
                                              op=OP.mult), waits=[tk_rms],
                 sig="v", fence=True)
            b.op(V, lambda e: e.tensor_scalar(out=t8b[:], in0=t8b[:], scalar1=1e-5,
                                              scalar2=None, op0=OP.max),
                 sig="v", fence=True)
            b.red(V, lambda e: e.reciprocal(out=t8c[:], in_=t8b[:]))
            b.op(V, lambda e: e.scalar_tensor_tensor(out=qhsc[:], in0=t8c[:],
                                                     scalar=127.0, in1=rmsr[:],
                                                     op0=OP.mult, op1=OP.mult),
                 sig="v", fence=True)
            tk_scl = b.op(V, lambda e: e.scalar_tensor_tensor(
                out=invd[:], in0=t8b[:], scalar=1.0 / 127.0,
                in1=swi[:, 2:3].broadcast_to((128, NQ)), op0=OP.mult, op1=OP.mult),
                sig="v")
            if DEBUG:
                b.dma(Y, dbg["ssqp"], ssqp[:], waits=[tk_scl])
                b.dma(Y, dbg["hlmp"], hlmp[:], waits=[tk_scl])
                b.dma(Y, dbg["scales"][:, 0:8], t8a[:], waits=[tk_scl])
                b.dma(Y, dbg["scales"][:, 8:16], rmsr[:], waits=[tk_scl])
                b.dma(Y, dbg["scales"][:, 16:24], t8c[:], waits=[tk_scl])
                b.dma(Y, dbg["scales"][:, 24:32], qhsc[:], waits=[tk_scl])
                b.dma(Y, dbg["scales"][:, 32:40], invd[:], waits=[tk_scl])
                b.dma(Y, dbg["scales"][:, 40:48], t8b[:], waits=[tk_scl])

            with ExitStack() as _ps9:
                out_acc = _ps9.enter_context(nc.sbuf_tensor("out_acc", [128, NQ * D], F32))
                with ExitStack() as _ps10:
                    ht20 = _ps10.enter_context(nc.sbuf_tensor("ht20", [128, 1024], F16))
                    ht21 = _ps10.enter_context(nc.sbuf_tensor("ht21", [128, 1024], F16))
                    z1k = _ps10.enter_context(nc.sbuf_tensor("z1k", [128, 1024], F32))
                    i1k = _ps10.enter_context(nc.sbuf_tensor("i1k", [128, 1024], I32))
                    qht = _ps10.enter_context(nc.sbuf_tensor("qht", [128, 1024], BF16))
                    qhTs = _ps10.enter_context(nc.sbuf_tensor("qhTs", [128, 8 * TOK], BF16))
                    dnscr = _ps10.enter_context(nc.sbuf_tensor("dnscr", [128, 2048], F32))
                    wdt = _ps10.enter_context(nc.sbuf_tensor("wdt", [128, 8 * 2048], BF16))
                    ptr = _ps10.enter_context(nc.psum_tensor("ptr", [128, 1024], BF16))
                    pdn = _ps10.enter_context(nc.psum_tensor("pdn", [128, 2048], F32))
                    htb2 = [ht20, ht21]
                    hfree = [None, None]
                    qhv = qhTs[:].rearrange("p (k j) -> p k j", k=8)
                    wdv = wdt[:].rearrange("p (k j) -> p k j", k=8)
                    mm_prev = None
                    ptr_ev = None
                    sev_prev = None
                    vadd_prev = None
                    for slab in range(8):
                        tevs = None
                        for q in range(NQ):
                            buf = q % 2
                            thl = b.dma(Y, htb2[buf][:],
                                        h_d.rearrange("(qq p) i -> p qq i", p=128)
                                        [:, q, slab * 1024:(slab + 1) * 1024],
                                        waits=mt(hfree[buf]))
                            b.op(V, (lambda bu, sl: lambda e: e.tensor_tensor(
                                out=z1k[:], in0=htb2[bu][:],
                                in1=lnwbc[:, sl * 1024:(sl + 1) * 1024],
                                op=OP.mult))(buf, slab),
                                waits=mt(thl, tk_scl if slab + q == 0 else None))
                            tz = b.op(V, (lambda q2: lambda e: e.tensor_scalar(
                                out=z1k[:], in0=z1k[:], scalar1=qhsc[:, q2:q2 + 1],
                                scalar2=None, op0=OP.mult))(q),
                                sig="v")
                            hfree[buf] = tz
                            if DEBUG and slab == 0 and q == 0:
                                b.dma(Y, dbg["ht2d"], htb2[buf][:], waits=[tz])
                                b.dma(Y, dbg["z1kd"], z1k[:], waits=[tz])
                            b.op(V, lambda e: e.tensor_copy(out=i1k[:], in_=z1k[:]))
                            tqh = b.op(V, lambda e: e.tensor_copy(out=qht[:], in_=i1k[:]),
                                       sig="v")
                            if DEBUG and slab == 0 and q == 0:
                                b.dma(Y, dbg["qhd"], qht[:], waits=[tqh])
                            ttr = None
                            for tt in range(8):
                                ttr = b.op(TE, (lambda t2: lambda e: e.transpose(
                                    ptr[:, t2 * 128:(t2 + 1) * 128],
                                    qht[:, t2 * 128:(t2 + 1) * 128], ident[:]))(tt),
                                    waits=mt(tqh if tt == 0 else None,
                                             ptr_ev if tt == 0 else None,
                                             mm_prev if tt == 0 and q == 0 else None),
                                    sig="t" if tt == 7 else None)
                            tevs = b.op(V, (lambda q2: lambda e: e.tensor_copy(
                                out=qhv[:, :, q2 * 128:(q2 + 1) * 128],
                                in_=ptr[:].rearrange("p (k j) -> p k j", k=8)))(q),
                                waits=[ttr], sig="v")
                            ptr_ev = tevs
                        twd = b.dma(Y, wdv, qd_out.rearrange("(a p) d -> p a d", p=128)
                                    [:, slab * 8:(slab + 1) * 8, :],
                                    waits=mt(agtok[2] if slab == 0 else None, mm_prev))
                        for q in range(NQ):
                            mmq = None
                            for dm in range(4):
                                for kI in range(8):
                                    mmq = b.op(TE, (lambda q2, dm2, k2: lambda e: e.matmul(
                                        pdn[:, dm2 * 512:(dm2 + 1) * 512],
                                        qhv[:, k2, q2 * 128:(q2 + 1) * 128],
                                        wdv[:, k2, dm2 * 512:(dm2 + 1) * 512],
                                        start=(k2 == 0), stop=(k2 == 7)))(q, dm, kI),
                                        waits=mt(tevs if dm + kI == 0 else None,
                                                 twd if dm + kI == 0 else None,
                                                 sev_prev if dm + kI == 0 else None),
                                        sig="t" if dm == 3 and kI == 7 else None)
                            mm_prev = mmq
                            sev = b.op(S, (lambda q2: lambda e: e.activation(
                                out=dnscr[:], in_=pdn[:], func=AF.Identity,
                                scale=invd[:, q2:q2 + 1]))(q),
                                waits=mt(mmq, vadd_prev), sig="s")
                            sev_prev = sev
                            if slab == 0:
                                vadd_prev = b.op(V, (lambda q2: lambda e: e.tensor_copy(
                                    out=out_acc[:, q2 * D:(q2 + 1) * D], in_=dnscr[:]))(q),
                                    waits=[sev], sig="v")
                            else:
                                vadd_prev = b.op(V, (lambda q2: lambda e: e.tensor_tensor(
                                    out=out_acc[:, q2 * D:(q2 + 1) * D],
                                    in0=out_acc[:, q2 * D:(q2 + 1) * D], in1=dnscr[:],
                                    op=OP.add))(q), waits=[sev], sig="v")
                    if DEBUG:
                        for q in range(NQ):
                            b.dma(Y, dbg["ffn"].rearrange("(qq p) d -> p qq d", p=128)
                                  [:, q, :], out_acc[:, q * D:(q + 1) * D],
                                  waits=[vadd_prev])
                    b.barrier()

                b.barrier()
                with ExitStack() as _ps11:
                    y2 = _ps11.enter_context(nc.sbuf_tensor("y2", [128, NS * TOK], BF16))
                    wotf0 = _ps11.enter_context(nc.sbuf_tensor("wotf0", [128, 512], F32))
                    wotf1 = _ps11.enter_context(nc.sbuf_tensor("wotf1", [128, 512], F32))
                    wot0 = _ps11.enter_context(nc.sbuf_tensor("wot0", [128, 512], BF16))
                    wot1 = _ps11.enter_context(nc.sbuf_tensor("wot1", [128, 512], BF16))
                    rel5 = _ps11.enter_context(nc.sbuf_tensor("rel5", [128, 512], F32))
                    xres = _ps11.enter_context(nc.sbuf_tensor("xres", [128, D], F32))
                    dxres = _ps11.enter_context(nc.sbuf_tensor("dxres", [128, D], BF16))
                    po0 = _ps11.enter_context(nc.psum_tensor("po0", [128, 512], F32))
                    po1 = _ps11.enter_context(nc.psum_tensor("po1", [128, 512], F32))
                    po2 = _ps11.enter_context(nc.psum_tensor("po2", [128, 512], F32))
                    po3 = _ps11.enter_context(nc.psum_tensor("po3", [128, 512], F32))
                    po4 = _ps11.enter_context(nc.psum_tensor("po4", [128, 512], F32))
                    po5 = _ps11.enter_context(nc.psum_tensor("po5", [128, 512], F32))
                    po6 = _ps11.enter_context(nc.psum_tensor("po6", [128, 512], F32))
                    po7 = _ps11.enter_context(nc.psum_tensor("po7", [128, 512], F32))
                    po = [po0, po1, po2, po3, po4, po5, po6, po7]
                    tk_y2 = b.dma(Y, y2[:].rearrange("p (s j) -> p s j", s=NS),
                                  y_d.rearrange("(s dp) j -> dp s j", dp=128),
                                  waits=[tysp])
                    wotb = [wot0, wot1]
                    wotf = [wotf0, wotf1]
                    wfree = [None, None]
                    wffree = [None, None]
                    vdone = {q: None for q in range(NQ)}
                    last_rel_rd = None
                    for dmc in range(4):
                        mml = {q: None for q in range(NQ)}
                        for s in range(NS):
                            buf = s % 2
                            twf = b.dma(Y, wotf[buf][:],
                                        woT_d[s * 128:(s + 1) * 128,
                                              dmc * 512:(dmc + 1) * 512],
                                        waits=mt(wffree[buf]))
                            twl = b.op(S, (lambda bu: lambda e: e.activation(
                                out=wotb[bu][:], in_=wotf[bu][:], func=AF.Copy))(buf),
                                waits=mt(twf, wfree[buf]), sig="s")
                            wffree[buf] = twl
                            for q in range(NQ):
                                mml[q] = b.op(TE, (lambda s2, q2, bu: lambda e: e.matmul(
                                    po[q2][:],
                                    y2[:, s2 * TOK + q2 * 128: s2 * TOK + (q2 + 1) * 128],
                                    wotb[bu][:], start=(s2 == 0), stop=False))(s, q, buf),
                                    waits=mt(twl if q == 0 else None,
                                             tk_y2 if dmc + s + q == 0 else None,
                                             vdone[q] if s == 0 else None),
                                    sig="t" if q == NQ - 1 else None)
                            wfree[buf] = mml[NQ - 1]
                        for q in range(NQ):
                            tmb = b.op(TE, (lambda q2, d2: lambda e: e.matmul(
                                po[q2][:], ones1[:], obrow_s[0:1, d2 * 512:(d2 + 1) * 512],
                                start=False, stop=True))(q, dmc),
                                waits=mt(tk_ob if dmc + q == 0 else None), sig="t")
                            srel = b.op(S, (lambda q2: lambda e: e.activation(
                                out=rel5[:], in_=po[q2][:], func=AF.Relu))(q),
                                waits=mt(tmb, last_rel_rd), sig="s")
                            vdone[q] = b.op(V, (lambda q2, d2: lambda e: e.tensor_tensor(
                                out=out_acc[:, q2 * D + d2 * 512: q2 * D + (d2 + 1) * 512],
                                in0=out_acc[:, q2 * D + d2 * 512: q2 * D + (d2 + 1) * 512],
                                in1=rel5[:], op=OP.add))(q, dmc), waits=[srel], sig="v")
                            last_rel_rd = vdone[q]
                    tx_prev = None
                    td_prev = None
                    for q in range(NQ):
                        tx = b.dma(Y, xres[:],
                                   x_d.rearrange("(qq p) d -> p qq d", p=128)[:, q, :],
                                   waits=mt(vdone[q], tx_prev))
                        td = b.dma(Y, dxres[:],
                                   dxn_d.rearrange("(qq p) d -> p qq d", p=128)[:, q, :],
                                   waits=mt(tk_dxn_dma if q == 0 else None, td_prev))
                        tx_prev = b.op(V, (lambda q2: lambda e: e.tensor_tensor(
                            out=out_acc[:, q2 * D:(q2 + 1) * D],
                            in0=out_acc[:, q2 * D:(q2 + 1) * D], in1=xres[:],
                            op=OP.add))(q), waits=mt(tx, vdone[q]), sig="v")
                        td_prev = b.op(V, (lambda q2: lambda e: e.tensor_tensor(
                            out=out_acc[:, q2 * D:(q2 + 1) * D],
                            in0=out_acc[:, q2 * D:(q2 + 1) * D], in1=dxres[:],
                            op=OP.add))(q), waits=[td], sig="v")
                        b.dma(Y, out_d.rearrange("(qq p) d -> p qq d", p=128)[:, q, :],
                              out_acc[:, q * D:(q + 1) * D], waits=[td_prev])

        with nc.Block() as blk:
            b.play(blk, sems)

    nc.all_engine_barrier()
    cleanup.__exit__(None, None, None)
    return nc


_prog_cache = {}


def _get_prog():
    if DEBUG not in _prog_cache:
        _prog_cache[DEBUG] = build_program()
    return _prog_cache[DEBUG]


def kernel(L0_attn_hidden_states, L0_attn_x_proj_weight, L0_attn_dt_proj_weight,
           L0_attn_dt_proj_bias, L0_attn_A_log, L0_attn_D, L0_attn_out_proj_weight,
           L0_attn_out_proj_bias, L0_ffn_w_gate, L0_ffn_w_up, L0_ffn_w_down,
           L0_ffn_ln_weight):
    hs = np.ascontiguousarray(np.asarray(L0_attn_hidden_states, dtype=np.float32))
    xpw = np.ascontiguousarray(np.asarray(L0_attn_x_proj_weight, dtype=np.float32))
    dtp = np.ascontiguousarray(np.asarray(L0_attn_dt_proj_weight, dtype=np.float32))
    dtb = np.ascontiguousarray(np.asarray(L0_attn_dt_proj_bias, dtype=np.float32))
    alog = np.ascontiguousarray(np.asarray(L0_attn_A_log, dtype=np.float32))
    Dv = np.ascontiguousarray(np.asarray(L0_attn_D, dtype=np.float32))
    wo = np.ascontiguousarray(np.asarray(L0_attn_out_proj_weight, dtype=np.float32))
    ob = np.ascontiguousarray(np.asarray(L0_attn_out_proj_bias, dtype=np.float32))
    wg = np.ascontiguousarray(np.asarray(L0_ffn_w_gate, dtype=np.float32))
    wu = np.ascontiguousarray(np.asarray(L0_ffn_w_up, dtype=np.float32))
    wd = np.ascontiguousarray(np.asarray(L0_ffn_w_down, dtype=np.float32))
    lnw = np.ascontiguousarray(np.asarray(L0_ffn_ln_weight, dtype=np.float32))

    xpwr = np.ascontiguousarray(xpw[0:RK, :])
    wbc = np.ascontiguousarray(xpw[RK:RK + 2 * N, :].T)
    dtpT = np.ascontiguousarray(dtp.T)
    dtb_rs = np.ascontiguousarray(dtb.reshape(NS, 128).T)
    alog_rs = np.ascontiguousarray(alog.reshape(NS, 128, N).transpose(1, 0, 2)
                                   .reshape(128, NS * N))
    woT = np.ascontiguousarray(wo.T)
    wgT = np.ascontiguousarray(wg.T)
    wuT = np.ascontiguousarray(wu.T)
    wdT = np.ascontiguousarray(wd.T)
    ident = np.eye(128, dtype=ml_dtypes.bfloat16)

    nc = _get_prog()
    in_maps = []
    for c in range(NCORES):
        mask = np.zeros((1, NCORES), np.float32)
        if c > 0:
            mask[0, c - 1] = 1.0
        in_maps.append({
            "x": np.ascontiguousarray(hs[:, c * T:(c + 1) * T, :].transpose(0, 1, 2)
                                      .reshape(TOK, D)),
            "xpwr": xpwr, "wbc": wbc, "dtpT": dtpT, "dtb": dtb_rs, "alog": alog_rs,
            "drow": Dv.reshape(1, D), "obrow": ob.reshape(1, D), "woT": woT,
            "lnw": lnw.reshape(1, II),
            "wg_sh": np.ascontiguousarray(wgT[c * (D // NCORES):(c + 1) * (D // NCORES), :]),
            "wu_sh": np.ascontiguousarray(wuT[c * (D // NCORES):(c + 1) * (D // NCORES), :]),
            "wd_sh": np.ascontiguousarray(wdT[c * (II // NCORES):(c + 1) * (II // NCORES), :]),
            "maskr": mask, "ident": ident,
            "constr": np.array([[1e-5, 1e-6, -0.5, -1.0, 0, 0, 0, 0]], np.float32),
        })
    res = run_bass_kernel_spmd(nc, in_maps, core_ids=list(range(NCORES)))
    kernel._last = res
    out = np.empty((B, L, D), np.float32)
    for c in range(NCORES):
        out[:, c * T:(c + 1) * T, :] = np.asarray(res.results[c]["out"]).reshape(B, T, D)
    return out
